# version 13
# baseline (speedup 1.0000x reference)
import numpy as np
import ml_dtypes
import concourse.bass as bass
import concourse.mybir as mybir
from concourse.bass_utils import run_bass_kernel_spmd
from contextlib import ExitStack

F32 = mybir.dt.float32
BF16 = mybir.dt.float16
AF = mybir.ActivationFunctionType
ALU = mybir.AluOpType
AX = mybir.AxisListType

P = 128
T = 2048
D = 1024
KC = 8
NCH = 16
NTS = 4
NH = 16
DH = 64
NSH = 4
NRT = 64
NEXP = NSH + NRT
FUP = 512
FUPC = 4
EPS_LN1 = 1e-5
EPS_LN2 = 1e-3
EPS_QK = 1e-6
GEXP = 8
SCALE_MOE = 2.5

N_UPT = NEXP * 16
N_DNT = (NEXP // 2) * 32
DEBUG_TAPS = False
DEBUG_EPI = 0


def routed(i):
    return i >= NSH


def build_nc():
    nc = bass.Bass("TRN2", target_bir_lowering=False)

    x_h = nc.dram_tensor("x", [T, D], F32, kind="ExternalInput").ap()
    W_h = {}
    for nm in ["Wq", "Wk", "Wv", "Wq2", "Wk2", "Wv2", "Wo"]:
        W_h[nm] = nc.dram_tensor(nm, [D, D], F32, kind="ExternalInput").ap()
    b_h = {}
    for nm in ["bq", "bk", "bv", "bq2", "bk2", "bv2", "bo", "gamma", "beta"]:
        b_h[nm] = nc.dram_tensor(nm, [D], F32, kind="ExternalInput").ap()
    rw_h = nc.dram_tensor("router_w", [NRT, D], F32, kind="ExternalInput").ap()
    cbr_h = nc.dram_tensor("corr_bias", [NRT], F32, kind="ExternalInput").ap()
    upw_h = nc.dram_tensor("up_w", [NRT, D, FUP], F32, kind="ExternalInput").ap()
    dnw_h = nc.dram_tensor("down_w", [NRT, FUP, D], F32, kind="ExternalInput").ap()
    sup_h = nc.dram_tensor("shared_up", [D, 4 * FUP], F32, kind="ExternalInput").ap()
    sdn_h = nc.dram_tensor("shared_down", [4 * FUP, D], F32, kind="ExternalInput").ap()
    onesb_h = nc.dram_tensor("ones_bf", [512], BF16, kind="ExternalInput").ap()
    onesf_h = nc.dram_tensor("ones_f", [P], F32, kind="ExternalInput").ap()
    out_h = nc.dram_tensor("out", [T, D], F32, kind="ExternalOutput").ap()
    if DEBUG_TAPS:
        dbg_xn = nc.dram_tensor("dbg_xn", [T, D], BF16, kind="ExternalOutput").ap()
        dbg_cfm = nc.dram_tensor("dbg_cfm", [P, T], BF16, kind="ExternalOutput").ap()
        dbg_g = nc.dram_tensor("dbg_g", [P, NCH, D], BF16, kind="ExternalOutput").ap()
        dbg_sbf = nc.dram_tensor("dbg_sbf", [P, 512], BF16, kind="ExternalOutput").ap()
        dbg_ao = nc.dram_tensor("dbg_ao", [P, KC, T], BF16, kind="ExternalOutput").ap()
        dbg_qk = nc.dram_tensor("dbg_qk", [P, NCH, NH], F32, kind="ExternalOutput").ap()
        dbg_q2 = nc.dram_tensor("dbg_q2", [P, NCH, D], BF16, kind="ExternalOutput").ap()
        dbg_k2 = nc.dram_tensor("dbg_k2", [P, NCH, D], BF16, kind="ExternalOutput").ap()
        dbg_ssb = nc.dram_tensor("dbg_ssb", [P, 512], F32, kind="ExternalOutput").ap()
        dbg_rtw = nc.dram_tensor("dbg_rtw", [P, KC, NRT], BF16, kind="ExternalOutput").ap()
        dbg_cbb = nc.dram_tensor("dbg_cbb", [P, NRT], F32, kind="ExternalOutput").ap()
        dbg_wr0 = nc.dram_tensor("dbg_wr0", [P, KC, FUP], BF16, kind="ExternalOutput").ap()
        dbg_pfm = nc.dram_tensor("dbg_pfm", [P, KC, T], BF16, kind="ExternalOutput").ap()

    dxn_h = nc.dram_tensor("dxn", [T, D], BF16, kind="Internal").ap()
    dctm_h = nc.dram_tensor("dctm", [T, P], BF16, kind="Internal").ap()
    dcfm_h = nc.dram_tensor("dcfm", [P, T], BF16, kind="Internal").ap()
    dao_h = nc.dram_tensor("dao", [T, D], BF16, kind="Internal").ap()
    drt_h = nc.dram_tensor("drt", [NRT, D], BF16, kind="Internal").ap()
    cc_in = nc.dram_tensor("cc_in", [P, 512], F32, kind="Internal")
    cc_out = nc.dram_tensor("cc_out", [P, 512], F32, kind="Internal")

    es = ExitStack()
    SB = lambda nm, sh, dt: es.enter_context(nc.sbuf_tensor(nm, sh, dt))

    xn = SB("xn", [P, KC, T], BF16)
    AR5 = SB("AR5", [33, T], F32)
    rtw = SB("rtw", [P, KC, NRT], BF16)
    cb_b = SB("cb_b", [P, NRT], F32)
    ones_bf = SB("onesb", [65, 512], BF16)
    ones_f = SB("onesf", [33, P], F32)
    pack1 = SB("pack1", [65, D], BF16)
    pack2 = SB("pack2", [65, D], BF16)
    borow = SB("borow", [1, D], BF16)
    mu1 = SB("mu1", [P, NCH], F32)
    r1 = SB("r1", [P, NCH], F32)
    stats = SB("stats", [P, 16], F32)
    scoreb = SB("scoreb", [P, 2, NRT], F32)
    ctmb = SB("ctmb", [P, 2, P], BF16)
    dcol = SB("dcol", [P, 2], F32)
    rcol = SB("rcol", [P, 2], F32)
    epsc = SB("epsc", [P, 4], F32)

    AR1 = SB("AR1", [P, NCH, D], F32)
    AR2 = SB("AR2", [P, 8192], F32)
    AR3 = SB("AR3", [P, 8192], F32)
    AR4 = SB("AR4", [P, 4096], F32)
    SCR = SB("SCR", [P, 2048], F32)

    def bsplit(ap, *dims):
        b = ap.bitcast(BF16)
        if len(b.shape) == 3:
            b = b.rearrange("p a b -> p (a b)")
        tot = b.shape[-1]
        if len(dims) == 1:
            assert dims[0] == tot
            return b
        if len(dims) == 2:
            return b.rearrange("p (a b) -> p a b", a=dims[0])
        if len(dims) == 3:
            return b.rearrange("p (a b c) -> p a b c", a=dims[0], b=dims[1])
        raise ValueError(dims)

    acc = AR1[:, :, :]
    xst = AR1[:, 0:4, :]
    xnt = bsplit(AR1[:, 4:5, :], 2, D)
    upc = bsplit(AR2[:, :], 2, FUPC, T)
    wup = bsplit(AR3[:, 0:4096], 2, KC, FUP)
    wdn = bsplit(AR3[:, 4096:8192], 2, FUPC, D)
    cbt = bsplit(AR4[:, 0:2048], 2, T)
    rtmp = AR4[:, 2048:3072].rearrange("p (b t) -> p b t", b=2)
    ttmp = AR4[:, 3072:4096].rearrange("p (b t) -> p b t", b=2)
    xsq = SCR[:, 1024:2048]
    r_sfc = SCR[:, 0:64]
    r_gsort = SCR[:, 64:128]
    r_gsc = SCR[:, 128:136]
    r_gss = SCR[:, 136:144]
    r_gm = SCR[:, 144:152]
    r_msk = SCR[:, 192:256]
    r_top8 = SCR[:, 256:264]
    r_tm = SCR[:, 264:328]
    r_sel = SCR[:, 328:392]
    crow = bsplit(AR5[0:1, :], 2, T)

    pfm = bsplit(AR1[:, 0:8, :], KC, T)
    q2tm = bsplit(AR1[:, 8:16, :], NCH, D)
    aotm = bsplit(AR1[:, 8:16, :], NCH, D)
    aofm = bsplit(AR3[:, :], KC, T)
    gtm = bsplit(AR2[:, :], NCH, D)
    k2tm = bsplit(AR3[:, :], NCH, D)
    wring = bsplit(AR4[:, :], 2, KC, FUP)
    gamb = SCR[:, 0:1024]
    betb = SCR[:, 1024:2048]
    scrA = SCR[:, 0:2048].rearrange("p (b t) -> p b t", b=4)
    qscr = SCR[:, 0:1024]
    ssb = SCR[:, 1024:1536]
    qks = SCR[:, 1024:1280].rearrange("p (a b) -> p a b", a=NCH)
    qksr = SCR[:, 1792:2048].rearrange("p (a b) -> p a b", a=NCH)
    cfm_tmp = bsplit(SCR[:, 0:1024], T)
    sbf = bsplit(SCR[:, 1536:1792], 512)
    sbf2 = bsplit(SCR[:, 1280:1792], 1024)
    xst2 = AR1[:, 0:2, :]
    yt = AR1[:, 2, :]
    scT = AR1[:, 3, :]
    scE = AR1[:, 4, :]
    rgt = AR1[:, 5, :]
    ot = AR1[:, 6:8, :]

    psA = es.enter_context(nc.psum_tensor("psA", [P, 8, 512], F32))
    ups = psA[:, 0:2, :]
    dns = psA[:, 2:4, :]

    sem_names = [
        "ld", "x1", "lnst", "lnr", "lnx", "xnw", "xnf", "rtb", "rtw_s", "cbb",
        "rtl", "rts", "rtd", "rtr", "rtc", "ctw", "ctr1", "cfm_s",
        "wu", "wd", "crw", "cbp", "cbe",
        "upr", "upf", "cm", "ubf", "dnr", "dnf", "pedn",
        "gsl", "awl", "ahc",
        "aqr", "aqf", "atr", "atf", "ate", "er1", "er2",
        "qkr", "qsc", "spr", "sev", "sdm", "col", "sbfs",
        "aor", "aof", "aotr", "gbl", "gbr", "gbe",
        "wor", "epy", "eps", "epr", "epo", "x2", "ost", "cst", "dbg", "fnc",
    ]
    S = {}
    for nm in sem_names:
        S[nm] = es.enter_context(nc.semaphore(nm))

    ATTN_MATS = ["Wq", "Wq2", "Wk", "Wk2", "Wv", "Wv2", "Wo"]

    def bias_fm_ops(mat, fc):
        tns, b = {"Wq": (pack1, 0), "Wk": (pack1, 32), "Wv": (pack1, 64),
                  "Wv2": (pack2, 64)}[mat]
        return tns[b:b + 1, P * fc:P * (fc + 1)], ones_bf[b:b + 1, 0:512]

    def bias_tm_ops(mat, sl):
        if mat == "Wo":
            return ones_bf[0:1, 0:P], borow[0:1, 512 * sl:512 * (sl + 1)]
        tns, b = {"Wq2": (pack2, 0), "Wk2": (pack2, 32)}[mat]
        return ones_bf[b:b + 1, 0:P], tns[b:b + 1, 512 * sl:512 * (sl + 1)]

    def ffn_up_src(i):
        if i < NSH:
            return sup_h[:, FUP * i:FUP * (i + 1)].rearrange("(c p) m -> p c m", p=P)
        return upw_h[i - NSH].rearrange("(c p) m -> p c m", p=P)

    def ffn_dn_src(i):
        if i < NSH:
            return sdn_h[FUP * i:FUP * (i + 1), :].rearrange("(c p) m -> p c m", p=P)
        return dnw_h[i - NSH].rearrange("(c p) m -> p c m", p=P)

    with nc.Block() as block:

        @block.sync
        def _(sync):
            sync.dma_start(out=ones_f[0:1, :], in_=onesf_h[None, :]).then_inc(S["ld"], 16)
            sync.dma_start(out=ones_f[32:33, :], in_=onesf_h[None, :]).then_inc(S["ld"], 16)
            for c in range(4):
                sync.dma_start(out=xst[:, c % 4, :], in_=x_h[P * c:P * (c + 1), :]).then_inc(S["x1"], 16)
            for c in range(NCH):
                sync.wait_ge(S["lnx"], c + 1)
                sync.dma_start(out=dxn_h[P * c:P * (c + 1), :], in_=xnt[:, c % 2, :]).then_inc(S["xnw"], 16)
                if c + 4 < NCH:
                    sync.dma_start(out=xst[:, (c + 4) % 4, :],
                                   in_=x_h[P * (c + 4):P * (c + 5), :]).then_inc(S["x1"], 16)
            for fc in range(KC):
                sync.dma_start_transpose(xn[:, fc, :], dxn_h[:, P * fc:P * (fc + 1)]).then_inc(S["xnf"], 16)
            sync.wait_ge(S["rtb"], 16)
            for fc in range(KC):
                sync.dma_start_transpose(rtw[:, fc, :], drt_h[:, P * fc:P * (fc + 1)]).then_inc(S["rtw_s"], 16)
            for c in range(NCH):
                sync.wait_ge(S["rtc"], c + 1)
                sync.dma_start(out=dctm_h[P * c:P * (c + 1), :], in_=ctmb[:, c % 2, :]).then_inc(S["ctw"], 16)
            sync.wait_ge(S["ctw"], 16 * NCH)
            sync.dma_start_transpose(cfm_tmp, dctm_h[:, :]).then_inc(S["ctr1"], 16)
            sync.wait_ge(S["ctr1"], 16)
            sync.dma_start(out=dcfm_h, in_=cfm_tmp).then_inc(S["cfm_s"], 16)
            sync.wait_ge(S["cbp"], NRT)
            sync.dma_start(out=AR5[0:1, 0:D], in_=b_h["gamma"][None, :]).then_inc(S["gbl"], 16)
            sync.dma_start(out=AR5[32:33, 0:D], in_=b_h["beta"][None, :]).then_inc(S["gbl"], 16)
            sync.wait_ge(S["sev"], 1)
            sync.dma_start(out=cc_in.ap(), in_=ssb).then_inc(S["sdm"], 16)
            for tc in range(NCH):
                sync.wait_ge(S["aof"], tc + 1)
                sync.dma_start(out=dao_h[P * tc:P * (tc + 1), :], in_=aotm[:, tc, :]).then_inc(S["ctw"], 16)
            sync.wait_ge(S["ctw"], 16 * (NCH + NCH))
            for fc in range(KC):
                sync.dma_start_transpose(aofm[:, fc, :], dao_h[:, P * fc:P * (fc + 1)]).then_inc(S["aotr"], 16)
            sync.wait_ge(S["aqr"], 128)
            for tc in range(2):
                sync.dma_start(out=xst2[:, tc % 2, :], in_=x_h[P * tc:P * (tc + 1), :]).then_inc(S["x2"], 16)
            for tc in range(NCH):
                sync.wait_ge(S["epo"], tc + 1)
                sync.dma_start(out=out_h[P * tc:P * (tc + 1), :], in_=ot[:, tc % 2, :]).then_inc(S["ost"], 16)
                if tc + 2 < NCH:
                    sync.dma_start(out=xst2[:, tc % 2, :],
                                   in_=x_h[P * (tc + 2):P * (tc + 3), :]).then_inc(S["x2"], 16)
            sync.wait_ge(S["ost"], 16 * NCH)

        @block.gpsimd
        def _(g):
            g.memset(epsc[:, 0:1], EPS_LN1)
            g.memset(epsc[:, 1:2], EPS_LN2)
            g.memset(epsc[:, 2:3], 1e-20).then_inc(S["cst"], 1)
            g.dma_start(out=ones_bf[0:1, :], in_=onesb_h[None, :]).then_inc(S["ld"], 16)
            g.dma_start(out=ones_bf[32:33, :], in_=onesb_h[None, :]).then_inc(S["ld"], 16)
            g.dma_start(out=ones_bf[64:65, :], in_=onesb_h[None, :]).then_inc(S["ld"], 16)
            g.dma_start(out=pack1[0:1, :], in_=b_h["bq"][None, :]).then_inc(S["ld"], 16)
            g.dma_start(out=pack1[32:33, :], in_=b_h["bk"][None, :]).then_inc(S["ld"], 16)
            g.dma_start(out=pack1[64:65, :], in_=b_h["bv"][None, :]).then_inc(S["ld"], 16)
            g.dma_start(out=pack2[0:1, :], in_=b_h["bq2"][None, :]).then_inc(S["ld"], 16)
            g.dma_start(out=pack2[32:33, :], in_=b_h["bk2"][None, :]).then_inc(S["ld"], 16)
            g.dma_start(out=pack2[64:65, :], in_=b_h["bv2"][None, :]).then_inc(S["ld"], 16)
            g.dma_start(out=borow[0:1, :], in_=b_h["bo"][None, :]).then_inc(S["ld"], 16)
            g.dma_start(out=scoreb[0:1, 0, :], in_=cbr_h[None, :]).then_inc(S["ld"], 16)
            g.dma_start(out=drt_h, in_=rw_h).then_inc(S["rtb"], 16)
            g.dma_start(out=wup[:, 0, :, :], in_=ffn_up_src(0)).then_inc(S["wu"], 16)
            g.dma_start(out=wdn[:, 0, :, :], in_=ffn_dn_src(0)).then_inc(S["wd"], 16)
            g.dma_start(out=wup[:, 1, :, :], in_=ffn_up_src(1)).then_inc(S["wu"], 16)

            u = 0
            for i in range(NEXP):
                if routed(i):
                    r = i - NSH
                    g.wait_ge(S["cbe"], r + 1)
                    for ti in range(16):
                        pi = u % 2
                        ts = ti // FUPC
                        g.wait_ge(S["upf"], u + 1)
                        if u >= 2:
                            g.wait_ge(S["ubf"], u - 1)
                        g.tensor_tensor(ttmp[:, pi, :], rtmp[:, pi, :],
                                        cbt[:, r % 2, 512 * ts:512 * (ts + 1)],
                                        ALU.mult).then_inc(S["cm"], 1)
                        u += 1
                else:
                    u += 16
                if i + 1 < NEXP and routed(i + 1):
                    rr = (i + 1) - NSH
                    if rr == 0:
                        g.wait_ge(S["cfm_s"], 16)
                    if rr >= 2:
                        g.wait_ge(S["cbp"], rr - 1)
                    g.dma_start(out=crow[0:1, rr % 2, :], in_=dcfm_h[rr:rr + 1, :]).then_inc(S["crw"], 16)
                if i + 2 < NEXP:
                    g.wait_ge(S["upr"], 16 * (i + 1))
                    g.dma_start(out=wup[:, (i + 2) % 2, :, :], in_=ffn_up_src(i + 2)).then_inc(S["wu"], 16)
                if i + 1 < NEXP:
                    if i >= 1:
                        g.wait_ge(S["pedn"], ((i - 1) | 1) + 1)
                    g.dma_start(out=wdn[:, (i + 1) % 2, :, :], in_=ffn_dn_src(i + 1)).then_inc(S["wd"], 16)

            g.wait_ge(S["ubf"], N_UPT)
            for H in range(14):
                mat = ATTN_MATS[H // 2]
                half = H % 2
                if H >= 2:
                    g.wait_ge(S["ahc"], H - 1)
                if DEBUG_TAPS and H == 2:
                    g.wait_ge(S["awl"], 16)
                    g.dma_start(out=dbg_wr0, in_=wring[:, 0, :, :]).then_inc(S["dbg"], 16)
                if DEBUG_TAPS and H == 4:
                    g.wait_ge(S["aqf"], 32)
                    g.dma_start(out=dbg_pfm, in_=pfm).then_inc(S["dbg"], 16)
                g.dma_start(out=wring[:, H % 2, :, :],
                            in_=W_h[mat][:, 512 * half:512 * (half + 1)].rearrange(
                                "(c p) m -> p c m", p=P)).then_inc(S["awl"], 16)
            if DEBUG_TAPS:
                g.wait_ge(S["ate"], 64)
                g.dma_start(out=dbg_q2, in_=q2tm).then_inc(S["dbg"], 16)
                g.dma_start(out=dbg_k2, in_=k2tm).then_inc(S["dbg"], 16)
            g.wait_ge(S["sdm"], 16)
            g.collective_compute("AllReduce", ALU.add, replica_groups=[list(range(8))],
                                 ins=[cc_in.ap().opt()], outs=[cc_out.ap().opt()]).then_inc(S["col"], 1)
            g.wait_ge(S["col"], 1)
            g.memset(sbf2, 0.0)
            g.dma_start(out=sbf2.rearrange("p (j c) -> p j c", j=8)[0:64, :, 0:64],
                        in_=cc_out.ap().rearrange("p (j c) -> p j c", j=8)[0:64, :, :]).then_inc(S["sbfs"], 16)
            g.dma_start(out=sbf2.rearrange("p (j c) -> p j c", j=8)[64:128, :, 64:128],
                        in_=cc_out.ap().rearrange("p (j c) -> p j c", j=8)[64:128, :, :]).then_inc(S["sbfs"], 16)
            if DEBUG_TAPS:
                g.wait_ge(S["epo"], NCH)
                g.dma_start(out=dbg_xn, in_=dxn_h).then_inc(S["dbg"], 16)
                g.dma_start(out=dbg_cfm, in_=dcfm_h).then_inc(S["dbg"], 16)
                g.dma_start(out=dbg_g, in_=gtm).then_inc(S["dbg"], 16)
                g.dma_start(out=dbg_sbf, in_=sbf).then_inc(S["dbg"], 16)
                g.dma_start(out=dbg_ao, in_=aofm).then_inc(S["dbg"], 16)
                g.dma_start(out=dbg_qk, in_=qksr).then_inc(S["dbg"], 16)
                g.dma_start(out=dbg_ssb, in_=ssb).then_inc(S["dbg"], 16)
                g.dma_start(out=dbg_rtw, in_=rtw[:, :, :]).then_inc(S["dbg"], 16)
                g.dma_start(out=dbg_cbb, in_=cb_b[:, :]).then_inc(S["dbg"], 16)
                g.wait_ge(S["dbg"], 208)

        @block.tensor
        def _(t):
            t.wait_ge(S["ld"], 16 * 13)
            t.matmul(psA[:, 7, 0:NRT], ones_f[0:1, :], scoreb[0:1, 0, :],
                     start=True, stop=True).then_inc(S["cbb"], 1)
            t.wait_ge(S["xnf"], 16 * KC)
            t.wait_ge(S["rtw_s"], 16 * KC)
            for c in range(NCH):
                if c == 1:
                    t.wait_ge(S["cbb"], 2)
                if c >= 2:
                    t.wait_ge(S["rts"], c - 1)
                for kc in range(KC):
                    mm = t.matmul(psA[:, 6 + (c % 2), 0:NRT],
                                  xn[:, kc, P * c:P * (c + 1)], rtw[:, kc, :],
                                  start=(kc == 0), stop=(kc == KC - 1))
                mm.then_inc(S["rtl"], 1)

            u = 0
            d = 0
            for i in range(NEXP):
                t.wait_ge(S["wu"], 16 * (i + 1))
                if routed(i):
                    r = i - NSH
                    t.wait_ge(S["crw"], 16 * (r + 1))
                    if r == 0:
                        t.wait_ge(S["rts"], NCH)
                        t.wait_ge(S["cbb"], 2)
                    if r >= 2:
                        t.wait_ge(S["cbe"], r - 1)
                    for s4 in range(4):
                        mm = t.matmul(psA[:, 4 + s4, :], ones_bf[0:1, 0:P],
                                      crow[0:1, r % 2, 512 * s4:512 * (s4 + 1)],
                                      start=True, stop=True)
                    mm.then_inc(S["cbp"], 1)
                def up_tile(ts, mc):
                    nonlocal u
                    pi = u % 2
                    if u >= 2:
                        t.wait_ge(S["upf"], u - 1)
                    for kc in range(KC):
                        mm = t.matmul(ups[:, pi, :],
                                      wup[:, i % 2, kc, P * mc:P * (mc + 1)],
                                      xn[:, kc, 512 * ts:512 * (ts + 1)],
                                      start=(kc == 0), stop=(kc == KC - 1))
                    mm.then_inc(S["upr"], 1)
                    u += 1

                def dn_tile(tc, sl):
                    nonlocal d
                    pi = d % 2
                    if d >= 2:
                        t.wait_ge(S["dnf"], d - 1)
                    first = True
                    for ee in (i - 1, i):
                        for kc in range(FUPC):
                            mm = t.matmul(dns[:, pi, :],
                                          upc[:, ee % 2, kc, P * tc:P * (tc + 1)],
                                          wdn[:, ee % 2, kc, 512 * sl:512 * (sl + 1)],
                                          start=first,
                                          stop=(ee == i and kc == FUPC - 1))
                            first = False
                    mm.then_inc(S["dnr"], 1)
                    d += 1

                if i % 2 == 0:
                    for ts in range(NTS):
                        for mc in range(FUPC):
                            up_tile(ts, mc)
                else:
                    t.wait_ge(S["wd"], 16 * (i + 1))
                    for g in range(NTS):
                        for mc in range(FUPC):
                            up_tile(g, mc)
                        t.wait_ge(S["ubf"], 16 * i + 4 * (g + 1))
                        for tc in range(4 * g, 4 * (g + 1)):
                            for sl in range(2):
                                dn_tile(tc, sl)
                    t.nop().then_inc(S["pedn"], 2)

            a_ctr = [0]
            at_ctr = [0]

            def fm_mat(mat, mat_i, rhs_t):
                for half in range(2):
                    H = 2 * mat_i + half
                    t.wait_ge(S["awl"], 16 * (H + 1))
                    for fc_l in range(4):
                        fc = 4 * half + fc_l
                        for ts in range(NTS):
                            a = a_ctr[0]
                            pi = a % 2
                            if a < 2:
                                t.wait_ge(S["upf"], N_UPT)
                            else:
                                t.wait_ge(S["aqf"], a - 1)
                            for kc in range(KC):
                                t.matmul(ups[:, pi, :],
                                         wring[:, H % 2, kc, P * fc_l:P * (fc_l + 1)],
                                         rhs_t[:, kc, 512 * ts:512 * (ts + 1)],
                                         start=(kc == 0), stop=False)
                            blh, brh = bias_fm_ops(mat, fc)
                            t.matmul(ups[:, pi, :], blh, brh, start=False,
                                     stop=True).then_inc(S["aqr"], 1)
                            a_ctr[0] += 1
                    t.nop().then_inc(S["ahc"], 1)

            def tm_mat(mat, mat_i, wait_fm_all):
                for half in range(2):
                    H = 2 * mat_i + half
                    t.wait_ge(S["awl"], 16 * (H + 1))
                    for tc in range(NCH):
                        at = at_ctr[0]
                        pi = at % 2
                        if at < 2:
                            t.wait_ge(S["dnf"], N_DNT)
                        else:
                            t.wait_ge(S["atf"], at - 1)
                        if tc == 0:
                            t.wait_ge(S["aqf"], wait_fm_all)
                        for kc in range(KC):
                            t.matmul(dns[:, pi, :],
                                     pfm[:, kc, P * tc:P * (tc + 1)],
                                     wring[:, H % 2, kc, :],
                                     start=(kc == 0), stop=False)
                        blh, brh = bias_tm_ops(mat, half)
                        t.matmul(dns[:, pi, :], blh, brh, start=False,
                                 stop=True).then_inc(S["atr"], 1)
                        at_ctr[0] += 1
                    t.nop().then_inc(S["ahc"], 1)

            fm_mat("Wq", 0, xn)
            tm_mat("Wq2", 1, 32)
            fm_mat("Wk", 2, xn)
            tm_mat("Wk2", 3, 64)
            fm_mat("Wv", 4, xn)

            t.wait_ge(S["cbe"], NRT)
            t.wait_ge(S["ate"], 64)
            for h in range(NH):
                for tc in range(NCH):
                    mm = t.matmul(psA[64 * (h % 2):64 * (h % 2) + 64, 4,
                                      DH * (h // 2):DH * (h // 2) + DH],
                                  k2tm[:, tc, DH * h:DH * (h + 1)],
                                  q2tm[:, tc, DH * h:DH * (h + 1)],
                                  start=(tc == 0), stop=(tc == NCH - 1))
            mm.then_inc(S["spr"], 1)

            fm_mat("Wv2", 5, pfm)

            t.wait_ge(S["sbfs"], 32)
            t.wait_ge(S["aqf"], 128)
            t.wait_ge(S["atf"], 64)
            for tc in range(NCH):
                if tc >= 2:
                    t.wait_ge(S["aof"], tc - 1)
                for j in range(8):
                    base = 5 if tc % 2 == 0 else 2
                    bidx = base + j // 4
                    col = (P * j) % 512
                    mm = t.matmul(psA[:, bidx, col:col + P],
                                  xn[:, j, P * tc:P * (tc + 1)],
                                  sbf2[:, P * j:P * (j + 1)],
                                  start=True, stop=True)
                mm.then_inc(S["aor"], 1)

            t.wait_ge(S["gbl"], 32)
            for gb in range(2):
                for sl in range(2):
                    t.matmul(psA[:, 7, :], ones_f[32 * gb:32 * gb + 1, :],
                             AR5[32 * gb:32 * gb + 1, 512 * sl:512 * (sl + 1)],
                             start=True, stop=True).then_inc(S["gbr"], 1)
                    t.wait_ge(S["gbe"], 2 * gb + sl + 1)

            t.wait_ge(S["awl"], 16 * 14)
            t.wait_ge(S["aotr"], 16 * KC)
            t.wait_ge(S["gbe"], 4)
            for tc in range(NCH):
                if tc >= 2:
                    t.wait_ge(S["epy"], tc - 1)
                for sl in range(2):
                    wo_bank = (ups[:, sl, :] if tc % 2 == 0
                               else psA[:, 4 + 3 * sl, :])
                    for kc in range(KC):
                        t.matmul(wo_bank,
                                 aofm[:, kc, P * tc:P * (tc + 1)],
                                 wring[:, (12 + sl) % 2, kc, :],
                                 start=(kc == 0), stop=False)
                    blh, brh = bias_tm_ops("Wo", sl)
                    t.matmul(wo_bank, blh, brh, start=False,
                             stop=True).then_inc(S["wor"], 1)

        @block.scalar
        def _(s):
            s.wait_ge(S["cst"], 1)
            for c in range(NCH):
                s.wait_ge(S["lnst"], c + 1)
                s.activation(stats[:, 0:1], r1[:, c:c + 1], AF.Ln, bias=epsc[:, 0:1])
                s.activation(r1[:, c:c + 1], stats[:, 0:1], AF.Exp, scale=-0.5).then_inc(S["lnr"], 1)
            for c in range(NCH):
                s.wait_ge(S["rtl"], c + 1)
                if c >= 2:
                    s.wait_ge(S["rtc"], c - 1)
                s.activation(scoreb[:, c % 2, :], psA[:, 6 + (c % 2), 0:NRT],
                             AF.Sigmoid).then_inc(S["rts"], 1)
                s.wait_ge(S["rtd"], c + 1)
                s.activation(stats[:, 1:2], dcol[:, c % 2:c % 2 + 1], AF.Ln, bias=epsc[:, 2:3])
                s.activation(rcol[:, c % 2:c % 2 + 1], stats[:, 1:2], AF.Exp,
                             scale=-1.0).then_inc(S["rtr"], 1)
            u = 0
            for i in range(NEXP):
                if routed(i):
                    r = i - NSH
                    s.wait_ge(S["cbp"], r + 1)
                    if r >= 2:
                        s.wait_ge(S["cm"], 16 * (r - 1))
                    for s4 in range(4):
                        cp = s.copy(cbt[:, r % 2, 512 * s4:512 * (s4 + 1)], psA[:, 4 + s4, :])
                    cp.then_inc(S["cbe"], 1)
                for ti in range(16):
                    pi = u % 2
                    s.wait_ge(S["upr"], u + 1)
                    if u >= 2:
                        s.wait_ge(S["ubf"], u - 1)
                    s.activation(rtmp[:, pi, :], ups[:, pi, :], AF.Relu).then_inc(S["upf"], 1)
                    u += 1
            s.wait_ge(S["dnf"], N_DNT)
            for tc in range(NCH):
                s.activation(gtm[:, tc, :], acc[:, tc, :], AF.Silu).then_inc(S["gsl"], 1)

            a_ctr = [0]
            t_ctr = [0]

            def fm_evac(tgt, pre_wait=None):
                for half in range(2):
                    for fc_l in range(4):
                        fc = 4 * half + fc_l
                        for ts in range(NTS):
                            a = a_ctr[0]
                            pi = a % 2
                            s.wait_ge(S["aqr"], a + 1)
                            if pre_wait is not None and half == 0 and fc_l == 0 and ts == 0:
                                s.wait_ge(S[pre_wait[0]], pre_wait[1])
                            s.copy(tgt[:, fc, 512 * ts:512 * (ts + 1)],
                                   ups[:, pi, :]).then_inc(S["aqf"], 1)
                            a_ctr[0] += 1

            def tm_elu_act():
                for k in range(32):
                    ti = t_ctr[0]
                    p2 = ti % 2
                    s.wait_ge(S["atr"], ti + 1)
                    if ti >= 2:
                        s.wait_ge(S["ate"], ti - 1)
                    s.activation(scrA[:, 2 * p2, :], dns[:, p2, :], AF.Relu).then_inc(S["er1"], 1)
                    s.wait_ge(S["atf"], ti + 1)
                    s.activation(scrA[:, 2 * p2 + 1, :], scrA[:, 2 * p2 + 1, :],
                                 AF.Exp).then_inc(S["er2"], 1)
                    t_ctr[0] += 1

            fm_evac(pfm)
            tm_elu_act()
            fm_evac(pfm, ("atr", 32))
            tm_elu_act()
            fm_evac(pfm, ("atr", 64))
            fm_evac(xn)

            for tc in range(NCH):
                s.wait_ge(S["eps"], tc + 1)
                s.activation(stats[:, 3:4], stats[:, 5:6], AF.Ln, bias=epsc[:, 1:2])
                s.activation(stats[:, 4:5], stats[:, 3:4], AF.Exp, scale=-0.5).then_inc(S["epr"], 1)

        @block.vector
        def _(v):
            fcnt = [0]

            def fence(op):
                fcnt[0] += 1
                op.then_inc(S["fnc"], 1)
                v.wait_ge(S["fnc"], fcnt[0])

            v.wait_ge(S["cbb"], 1)
            v.tensor_copy(cb_b[:, :], psA[:, 7, 0:NRT]).then_inc(S["cbb"], 1)
            for c in range(NCH):
                v.wait_ge(S["x1"], 16 * (c + 1))
                xc = xst[:, c % 4, :]
                v.reduce_sum(stats[:, 8:9], xc, axis=AX.X)
                fence(v.scalar_tensor_tensor(xsq, xc, 1.0, xc, ALU.mult, ALU.mult,
                                             accum_out=stats[:, 9:10]))
                v.tensor_scalar_mul(mu1[:, c:c + 1], stats[:, 8:9], 1.0 / D)
                v.tensor_scalar_mul(stats[:, 10:11], stats[:, 9:10], 1.0 / D)
                fence(v.tensor_tensor(stats[:, 11:12], mu1[:, c:c + 1], mu1[:, c:c + 1], ALU.mult))
                v.tensor_tensor(r1[:, c:c + 1], stats[:, 10:11], stats[:, 11:12],
                                ALU.subtract).then_inc(S["lnst"], 1)
                v.wait_ge(S["lnr"], c + 1)
                if c >= 2:
                    v.wait_ge(S["xnw"], 16 * (c - 1))
                v.tensor_scalar(xnt[:, c % 2, :], xc, mu1[:, c:c + 1], r1[:, c:c + 1],
                                ALU.subtract, ALU.mult).then_inc(S["lnx"], 1)
            for c in range(NCH):
                v.wait_ge(S["rts"], c + 1)
                sc = scoreb[:, c % 2, :]
                fence(v.tensor_tensor(r_sfc, sc, cb_b[:, :], ALU.add))
                for gi in range(GEXP):
                    op = v.max(r_gsort[:, 8 * gi:8 * gi + 8], r_sfc[:, 8 * gi:8 * gi + 8])
                fence(op)
                gs3 = r_gsort.rearrange("p (g j) -> p g j", g=GEXP)
                fence(v.tensor_tensor(r_gsc, gs3[:, :, 0], gs3[:, :, 1], ALU.add))
                fence(v.max(r_gss, r_gsc))
                fence(v.tensor_single_scalar(r_gm, r_gsc, r_gss[:, 3:4], ALU.is_ge))
                for gi in range(GEXP):
                    op = v.tensor_single_scalar(r_msk[:, 8 * gi:8 * gi + 8],
                                                r_sfc[:, 8 * gi:8 * gi + 8],
                                                r_gm[:, gi:gi + 1], ALU.mult)
                fence(op)
                fence(v.max(r_top8, r_msk))
                fence(v.tensor_single_scalar(r_tm, r_msk, r_top8[:, 7:8], ALU.is_ge))
                fence(v.tensor_tensor(r_sel, r_tm, sc, ALU.mult))
                v.reduce_sum(dcol[:, c % 2:c % 2 + 1], r_sel, axis=AX.X).then_inc(S["rtd"], 1)
                v.wait_ge(S["rtr"], c + 1)
                if c >= 2:
                    v.wait_ge(S["ctw"], 16 * (c - 1))
                v.memset(ctmb[:, c % 2, NRT:P], 0.0)
                v.tensor_scalar(ctmb[:, c % 2, 0:NRT], r_sel, rcol[:, c % 2:c % 2 + 1],
                                SCALE_MOE, ALU.mult, ALU.mult).then_inc(S["rtc"], 1)

            v.wait_ge(S["xnw"], 16 * NCH)
            v.memset(acc, 0.0)

            u = 0
            d = 0
            for i in range(NEXP):
                rt = routed(i)

                def sq_tile(ti):
                    nonlocal u
                    ts, mc = ti // FUPC, ti % FUPC
                    pi = u % 2
                    if ti == 0 and i >= 2:
                        v.wait_ge(S["pedn"], i - (i % 2))
                    if rt:
                        v.wait_ge(S["cm"], 16 * (i - NSH) + ti + 1)
                        src2 = ttmp[:, pi, :]
                    else:
                        v.wait_ge(S["upf"], u + 1)
                        src2 = rtmp[:, pi, :]
                    v.tensor_tensor(upc[:, i % 2, mc, 512 * ts:512 * (ts + 1)],
                                    rtmp[:, pi, :], src2, ALU.mult).then_inc(S["ubf"], 1)
                    u += 1

                def acc_tile(tc, sl):
                    nonlocal d
                    pi = d % 2
                    v.wait_ge(S["dnr"], d + 1)
                    v.tensor_tensor(acc[:, tc, 512 * sl:512 * (sl + 1)],
                                    acc[:, tc, 512 * sl:512 * (sl + 1)],
                                    dns[:, pi, :], ALU.add).then_inc(S["dnf"], 1)
                    d += 1

                if i % 2 == 0:
                    for ti in range(16):
                        sq_tile(ti)
                else:
                    for g in range(NTS):
                        for mc in range(FUPC):
                            sq_tile(g * FUPC + mc)
                        for tc in range(4 * g, 4 * (g + 1)):
                            for sl in range(2):
                                acc_tile(tc, sl)

            for ti in range(64):
                p2 = ti % 2
                if ti < 32:
                    tgt, sl, tc = q2tm, ti // 16, ti % 16
                else:
                    tgt, sl, tc = k2tm, (ti - 32) // 16, (ti - 32) % 16
                v.wait_ge(S["er1"], ti + 1)
                v.tensor_tensor(scrA[:, 2 * p2 + 1, :], dns[:, p2, :], scrA[:, 2 * p2, :],
                                ALU.subtract).then_inc(S["atf"], 1)
                v.wait_ge(S["er2"], ti + 1)
                v.tensor_tensor(tgt[:, tc, 512 * sl:512 * (sl + 1)],
                                scrA[:, 2 * p2, :], scrA[:, 2 * p2 + 1, :],
                                ALU.add).then_inc(S["ate"], 1)

            v.wait_ge(S["ate"], 64)
            for tc in range(NCH):
                fence(v.tensor_tensor(qscr, q2tm[:, tc, :], k2tm[:, tc, :], ALU.mult))
                for h in range(NH):
                    op = v.reduce_sum(qks[:, tc, h:h + 1], qscr[:, DH * h:DH * (h + 1)],
                                      axis=AX.X)
            fence(op)
            v.tensor_scalar_add(qks.rearrange("p a b -> p (a b)"),
                                qks.rearrange("p a b -> p (a b)"), EPS_QK)
            fence(v.reciprocal(qksr.rearrange("p a b -> p (a b)"),
                               qks.rearrange("p a b -> p (a b)")))
            v.wait_ge(S["spr"], 1)
            v.tensor_copy(ssb, psA[:, 4, :]).then_inc(S["sev"], 1)
            for tc in range(NCH):
                v.wait_ge(S["aor"], tc + 1)
                bank = psA[:, 5:7, :] if tc % 2 == 0 else psA[:, 2:4, :]
                bank = bank.rearrange("p a b -> p (a b)")
                for h in range(NH):
                    op = v.tensor_single_scalar(aotm[:, tc, DH * h:DH * (h + 1)],
                                                bank[:, DH * h:DH * (h + 1)],
                                                qksr[:, tc, h:h + 1], ALU.mult)
                op.then_inc(S["aof"], 1)
            v.wait_ge(S["sdm"], 16)
            for gb in range(2):
                tgt = gamb if gb == 0 else betb
                for sl in range(2):
                    v.wait_ge(S["gbr"], 2 * gb + sl + 1)
                    v.tensor_copy(tgt[:, 512 * sl:512 * (sl + 1)], psA[:, 7, :]).then_inc(S["gbe"], 1)

            for tc in range(NCH):
                v.wait_ge(S["wor"], 2 * (tc + 1))
                v.wait_ge(S["x2"], 16 * (tc + 1))
                fence(v.tensor_scalar(scT, xst2[:, tc % 2, :], mu1[:, tc:tc + 1], r1[:, tc:tc + 1],
                                      ALU.subtract, ALU.mult))
                if tc % 2 == 0:
                    wo_ps = psA[:, 0:2, :].rearrange("p a b -> p (a b)")
                else:
                    wo_ps = None
                if tc % 2 == 0:
                    v.tensor_tensor(yt, scT, wo_ps, ALU.add).then_inc(S["epy"], 1)
                else:
                    v.tensor_tensor(yt[0:P, 0:512], scT[0:P, 0:512], psA[:, 4, :], ALU.add)
                    v.tensor_tensor(yt[0:P, 512:1024], scT[0:P, 512:1024],
                                    psA[:, 7, :], ALU.add).then_inc(S["epy"], 1)
                v.wait_ge(S["epy"], tc + 1)
                v.reduce_sum(stats[:, 6:7], yt, axis=AX.X)
                fence(v.scalar_tensor_tensor(scE, yt, 1.0, yt, ALU.mult, ALU.mult,
                                             accum_out=stats[:, 7:8]))
                v.tensor_scalar_mul(stats[:, 12:13], stats[:, 6:7], 1.0 / D)
                v.tensor_scalar_mul(stats[:, 13:14], stats[:, 7:8], 1.0 / D)
                v.tensor_tensor(stats[:, 14:15], stats[:, 12:13], stats[:, 12:13], ALU.mult)
                v.tensor_tensor(stats[:, 5:6], stats[:, 13:14], stats[:, 14:15],
                                ALU.subtract).then_inc(S["eps"], 1)
                v.wait_ge(S["epr"], tc + 1)
                if tc == 0:
                    v.wait_ge(S["gbe"], 4)
                    v.wait_ge(S["gsl"], NCH)
                fence(v.tensor_single_scalar(rgt, gamb, stats[:, 4:5], ALU.mult))
                fence(v.scalar_tensor_tensor(scT, yt, stats[:, 12:13], rgt, ALU.subtract, ALU.mult))
                fence(v.tensor_tensor(scT, scT, betb, ALU.add))
                fence(v.tensor_tensor(scE, scT, gtm[:, tc, :], ALU.mult))
                if tc >= 2:
                    v.wait_ge(S["ost"], 16 * (tc - 1))
                if DEBUG_EPI == 1:
                    v.tensor_copy(ot[:, tc % 2, :], yt).then_inc(S["epo"], 1)
                elif DEBUG_EPI == 2:
                    v.tensor_copy(ot[:, tc % 2, :], scT).then_inc(S["epo"], 1)
                else:
                    v.tensor_tensor(ot[:, tc % 2, :], scE, xst2[:, tc % 2, :],
                                    ALU.add).then_inc(S["epo"], 1)

    es.close()
    return nc


_NC_CACHE = None


def kernel(L0_attn_x, L0_attn_Wq, L0_attn_bq, L0_attn_Wk, L0_attn_bk, L0_attn_Wv,
           L0_attn_bv, L0_attn_Wq2, L0_attn_bq2, L0_attn_Wk2, L0_attn_bk2,
           L0_attn_Wv2, L0_attn_bv2, L0_attn_Wo, L0_attn_bo, L0_attn_gamma,
           L0_attn_beta, L0_ffn_router_w, L0_ffn_correction_bias, L0_ffn_up_w,
           L0_ffn_down_w, L0_ffn_shared_up_w, L0_ffn_shared_down_w):
    global _NC_CACHE
    if _NC_CACHE is None:
        _NC_CACHE = build_nc()
    nc = _NC_CACHE

    f32c = lambda a: np.ascontiguousarray(np.asarray(a, dtype=np.float32))
    xfull = f32c(L0_attn_x).reshape(8 * T, D)
    com = {
        "Wq": f32c(L0_attn_Wq), "Wk": f32c(L0_attn_Wk), "Wv": f32c(L0_attn_Wv),
        "Wq2": f32c(L0_attn_Wq2), "Wk2": f32c(L0_attn_Wk2), "Wv2": f32c(L0_attn_Wv2),
        "Wo": f32c(L0_attn_Wo),
        "bq": f32c(L0_attn_bq), "bk": f32c(L0_attn_bk), "bv": f32c(L0_attn_bv),
        "bq2": f32c(L0_attn_bq2), "bk2": f32c(L0_attn_bk2), "bv2": f32c(L0_attn_bv2),
        "bo": f32c(L0_attn_bo), "gamma": f32c(L0_attn_gamma), "beta": f32c(L0_attn_beta),
        "router_w": f32c(L0_ffn_router_w), "corr_bias": f32c(L0_ffn_correction_bias),
        "up_w": f32c(L0_ffn_up_w), "down_w": f32c(L0_ffn_down_w),
        "shared_up": f32c(L0_ffn_shared_up_w), "shared_down": f32c(L0_ffn_shared_down_w),
        "ones_bf": np.ones((512,), dtype=np.float16),
        "ones_f": np.ones((P,), dtype=np.float32),
    }
    in_maps = []
    for core in range(8):
        m = dict(com)
        m["x"] = np.ascontiguousarray(xfull[T * core:T * (core + 1)])
        in_maps.append(m)
    res = run_bass_kernel_spmd(nc, in_maps, core_ids=list(range(8)))
    out = np.concatenate([np.asarray(r["out"]) for r in res.results], axis=0)
    return out.reshape(1, 8 * T, D).astype(np.float32)


# version 14 (final)
# speedup vs baseline: 1.1503x; 1.1503x over previous
import numpy as np
import ml_dtypes
import concourse.bass as bass
import concourse.mybir as mybir
from concourse.bass_utils import run_bass_kernel_spmd
from contextlib import ExitStack

F32 = mybir.dt.float32
BF16 = mybir.dt.float16
AF = mybir.ActivationFunctionType
ALU = mybir.AluOpType
AX = mybir.AxisListType

P = 128
T = 2048
D = 1024
KC = 8
NCH = 16
NTS = 4
NH = 16
DH = 64
NSH = 4
NRT = 64
NEXP = NSH + NRT
FUP = 512
FUPC = 4
EPS_LN1 = 1e-5
EPS_LN2 = 1e-3
EPS_QK = 1e-6
GEXP = 8
SCALE_MOE = 2.5

N_UPT = NEXP * 16
N_DNT = (NEXP // 2) * 32
DEBUG_TAPS = False
DEBUG_EPI = 0


def routed(i):
    return i >= NSH


def build_nc():
    nc = bass.Bass("TRN2", target_bir_lowering=False)

    x_h = nc.dram_tensor("x", [T, D], F32, kind="ExternalInput").ap()
    W_h = {}
    for nm in ["Wq", "Wk", "Wv", "Wq2", "Wk2", "Wv2", "Wo"]:
        W_h[nm] = nc.dram_tensor(nm, [D, D], F32, kind="ExternalInput").ap()
    b_h = {}
    for nm in ["bq", "bk", "bv", "bq2", "bk2", "bv2", "bo", "gamma", "beta"]:
        b_h[nm] = nc.dram_tensor(nm, [D], F32, kind="ExternalInput").ap()
    rw_h = nc.dram_tensor("router_w", [NRT, D], F32, kind="ExternalInput").ap()
    cbr_h = nc.dram_tensor("corr_bias", [NRT], F32, kind="ExternalInput").ap()
    upw_h = nc.dram_tensor("up_w", [NRT, D, FUP], F32, kind="ExternalInput").ap()
    dnw_h = nc.dram_tensor("down_w", [NRT, FUP, D], F32, kind="ExternalInput").ap()
    sup_h = nc.dram_tensor("shared_up", [D, 4 * FUP], F32, kind="ExternalInput").ap()
    sdn_h = nc.dram_tensor("shared_down", [4 * FUP, D], F32, kind="ExternalInput").ap()
    onesb_h = nc.dram_tensor("ones_bf", [512], BF16, kind="ExternalInput").ap()
    onesf_h = nc.dram_tensor("ones_f", [P], F32, kind="ExternalInput").ap()
    out_h = nc.dram_tensor("out", [T, D], F32, kind="ExternalOutput").ap()
    if DEBUG_TAPS:
        dbg_xn = nc.dram_tensor("dbg_xn", [T, D], BF16, kind="ExternalOutput").ap()
        dbg_cfm = nc.dram_tensor("dbg_cfm", [P, T], BF16, kind="ExternalOutput").ap()
        dbg_g = nc.dram_tensor("dbg_g", [P, NCH, D], BF16, kind="ExternalOutput").ap()
        dbg_sbf = nc.dram_tensor("dbg_sbf", [P, 512], BF16, kind="ExternalOutput").ap()
        dbg_ao = nc.dram_tensor("dbg_ao", [P, KC, T], BF16, kind="ExternalOutput").ap()
        dbg_qk = nc.dram_tensor("dbg_qk", [P, NCH, NH], F32, kind="ExternalOutput").ap()
        dbg_q2 = nc.dram_tensor("dbg_q2", [P, NCH, D], BF16, kind="ExternalOutput").ap()
        dbg_k2 = nc.dram_tensor("dbg_k2", [P, NCH, D], BF16, kind="ExternalOutput").ap()
        dbg_ssb = nc.dram_tensor("dbg_ssb", [P, 512], F32, kind="ExternalOutput").ap()
        dbg_rtw = nc.dram_tensor("dbg_rtw", [P, KC, NRT], BF16, kind="ExternalOutput").ap()
        dbg_cbb = nc.dram_tensor("dbg_cbb", [P, NRT], F32, kind="ExternalOutput").ap()
        dbg_wr0 = nc.dram_tensor("dbg_wr0", [P, KC, FUP], BF16, kind="ExternalOutput").ap()
        dbg_pfm = nc.dram_tensor("dbg_pfm", [P, KC, T], BF16, kind="ExternalOutput").ap()

    dxn_h = nc.dram_tensor("dxn", [T, D], BF16, kind="Internal").ap()
    dctm_h = nc.dram_tensor("dctm", [T, P], BF16, kind="Internal").ap()
    dcfm_h = nc.dram_tensor("dcfm", [P, T], BF16, kind="Internal").ap()
    dao_h = nc.dram_tensor("dao", [T, D], BF16, kind="Internal").ap()
    drt_h = nc.dram_tensor("drt", [NRT, D], BF16, kind="Internal").ap()
    cc_in = nc.dram_tensor("cc_in", [P, 512], F32, kind="Internal")
    cc_out = nc.dram_tensor("cc_out", [P, 512], F32, kind="Internal")

    es = ExitStack()
    SB = lambda nm, sh, dt: es.enter_context(nc.sbuf_tensor(nm, sh, dt))

    xn = SB("xn", [P, KC, T], BF16)
    AR5 = SB("AR5", [33, T], F32)
    rtw = SB("rtw", [P, KC, NRT], BF16)
    cb_b = SB("cb_b", [P, NRT], F32)
    ones_bf = SB("onesb", [65, 512], BF16)
    ones_f = SB("onesf", [33, P], F32)
    pack1 = SB("pack1", [65, D], BF16)
    pack2 = SB("pack2", [65, D], BF16)
    borow = SB("borow", [1, D], BF16)
    mu1 = SB("mu1", [P, NCH], F32)
    r1 = SB("r1", [P, NCH], F32)
    stats = SB("stats", [P, 16], F32)
    scoreb = SB("scoreb", [P, 2, NRT], F32)
    ctmb = SB("ctmb", [P, 2, P], BF16)
    dcol = SB("dcol", [P, 2], F32)
    rcol = SB("rcol", [P, 2], F32)
    epsc = SB("epsc", [P, 4], F32)

    AR1 = SB("AR1", [P, NCH, D], F32)
    AR2 = SB("AR2", [P, 8192], F32)
    AR3 = SB("AR3", [P, 8192], F32)
    AR4 = SB("AR4", [P, 4096], F32)
    SCR = SB("SCR", [P, 2048], F32)

    def bsplit(ap, *dims):
        b = ap.bitcast(BF16)
        if len(b.shape) == 3:
            b = b.rearrange("p a b -> p (a b)")
        tot = b.shape[-1]
        if len(dims) == 1:
            assert dims[0] == tot
            return b
        if len(dims) == 2:
            return b.rearrange("p (a b) -> p a b", a=dims[0])
        if len(dims) == 3:
            return b.rearrange("p (a b c) -> p a b c", a=dims[0], b=dims[1])
        raise ValueError(dims)

    acc = AR1[:, :, :]
    xst = AR1[:, 0:4, :]
    xnt = bsplit(AR1[:, 4:5, :], 2, D)
    upc = bsplit(AR2[:, :], 2, FUPC, T)
    wup = bsplit(AR3[:, 0:4096], 2, KC, FUP)
    wdn = bsplit(AR3[:, 4096:8192], 2, FUPC, D)
    cbt = bsplit(AR4[:, 0:2048], 2, T)
    rtmp = AR4[:, 2048:3072].rearrange("p (b t) -> p b t", b=2)
    ttmp = AR4[:, 3072:4096].rearrange("p (b t) -> p b t", b=2)
    xsq = SCR[:, 1024:2048]
    r_sfc = SCR[:, 0:64]
    r_gsort = SCR[:, 64:128]
    r_gsc = SCR[:, 128:136]
    r_gss = SCR[:, 136:144]
    r_gm = SCR[:, 144:152]
    r_msk = SCR[:, 192:256]
    r_top8 = SCR[:, 256:264]
    r_tm = SCR[:, 264:328]
    r_sel = SCR[:, 328:392]
    crow = bsplit(AR5[0:1, :], 2, T)

    pfm = bsplit(AR1[:, 0:8, :], KC, T)
    q2tm = bsplit(AR1[:, 8:16, :], NCH, D)
    aotm = bsplit(AR1[:, 8:16, :], NCH, D)
    aofm = bsplit(AR3[:, :], KC, T)
    gtm = bsplit(AR2[:, :], NCH, D)
    k2tm = bsplit(AR3[:, :], NCH, D)
    wring = bsplit(AR4[:, :], 2, KC, FUP)
    gamb = SCR[:, 0:1024]
    betb = SCR[:, 1024:2048]
    scrA = SCR[:, 0:2048].rearrange("p (b t) -> p b t", b=4)
    qscr = SCR[:, 0:1024]
    ssb = SCR[:, 1024:1536]
    qks = SCR[:, 1024:1280].rearrange("p (a b) -> p a b", a=NCH)
    qksr = SCR[:, 1792:2048].rearrange("p (a b) -> p a b", a=NCH)
    cfm_tmp = bsplit(SCR[:, 0:1024], T)
    sbf = bsplit(SCR[:, 1536:1792], 512)
    sbf2 = bsplit(SCR[:, 1280:1792], 1024)
    xst2 = AR1[:, 0:2, :]
    yt = AR1[:, 2, :]
    scT = AR1[:, 3, :]
    scE = AR1[:, 4, :]
    rgt = AR1[:, 5, :]
    ot = AR1[:, 6:8, :]

    psA = es.enter_context(nc.psum_tensor("psA", [P, 8, 512], F32))
    ups = psA[:, 0:2, :]
    dns = psA[:, 2:4, :]

    sem_names = [
        "ld", "ldh", "x1", "lnst", "lnr", "lnx", "xnw", "xnf", "rtb", "rtw_s", "cbb",
        "rtl", "rts", "rtd", "rtr", "rtc", "ctw", "ctr1", "cfm_s",
        "wu", "wd", "crw", "cbp", "cbe",
        "upr", "upf", "cm", "ubf", "dnr", "dnf", "pedn",
        "gsl", "awl", "ahc",
        "aqr", "aqf", "atr", "atf", "ate", "er1", "er2",
        "qkr", "qsc", "spr", "sev", "sdm", "col", "sbfs",
        "aor", "aof", "aotr", "gbl", "gbr", "gbe",
        "wor", "epy", "eps", "epr", "epo", "x2", "ost", "cst", "dbg", "fnc",
    ]
    S = {}
    for nm in sem_names:
        S[nm] = es.enter_context(nc.semaphore(nm))

    ATTN_MATS = ["Wq", "Wq2", "Wk", "Wk2", "Wv", "Wv2", "Wo"]

    def bias_fm_ops(mat, fc):
        tns, b = {"Wq": (pack1, 0), "Wk": (pack1, 32), "Wv": (pack1, 64),
                  "Wv2": (pack2, 64)}[mat]
        return tns[b:b + 1, P * fc:P * (fc + 1)], ones_bf[b:b + 1, 0:512]

    def bias_tm_ops(mat, sl):
        if mat == "Wo":
            return ones_bf[0:1, 0:P], borow[0:1, 512 * sl:512 * (sl + 1)]
        tns, b = {"Wq2": (pack2, 0), "Wk2": (pack2, 32)}[mat]
        return ones_bf[b:b + 1, 0:P], tns[b:b + 1, 512 * sl:512 * (sl + 1)]

    def ffn_up_src(i):
        if i < NSH:
            return sup_h[:, FUP * i:FUP * (i + 1)].rearrange("(c p) m -> p c m", p=P)
        return upw_h[i - NSH].rearrange("(c p) m -> p c m", p=P)

    def ffn_dn_src(i):
        if i < NSH:
            return sdn_h[FUP * i:FUP * (i + 1), :].rearrange("(c p) m -> p c m", p=P)
        return dnw_h[i - NSH].rearrange("(c p) m -> p c m", p=P)

    with nc.Block() as block:

        @block.sync
        def _(sync):
            sync.dma_start(out=ones_f[0:1, :], in_=onesf_h[None, :]).then_inc(S["ldh"], 16)
            sync.dma_start(out=ones_f[32:33, :], in_=onesf_h[None, :]).then_inc(S["ldh"], 16)
            for c in range(4):
                sync.dma_start(out=xst[:, c % 4, :], in_=x_h[P * c:P * (c + 1), :]).then_inc(S["x1"], 16)
            for c in range(NCH):
                sync.wait_ge(S["lnx"], c + 1)
                sync.dma_start(out=dxn_h[P * c:P * (c + 1), :], in_=xnt[:, c % 2, :]).then_inc(S["xnw"], 16)
                if c + 4 < NCH:
                    sync.dma_start(out=xst[:, (c + 4) % 4, :],
                                   in_=x_h[P * (c + 4):P * (c + 5), :]).then_inc(S["x1"], 16)
            for fc in range(KC):
                sync.dma_start_transpose(xn[:, fc, :], dxn_h[:, P * fc:P * (fc + 1)]).then_inc(S["xnf"], 16)
            sync.wait_ge(S["rtb"], 16)
            for fc in range(KC):
                sync.dma_start_transpose(rtw[:, fc, :], drt_h[:, P * fc:P * (fc + 1)]).then_inc(S["rtw_s"], 16)
            for c in range(NCH):
                sync.wait_ge(S["rtc"], c + 1)
                sync.dma_start(out=dctm_h[P * c:P * (c + 1), :], in_=ctmb[:, c % 2, :]).then_inc(S["ctw"], 16)
            sync.wait_ge(S["ctw"], 16 * NCH)
            sync.dma_start_transpose(cfm_tmp, dctm_h[:, :]).then_inc(S["ctr1"], 16)
            sync.wait_ge(S["ctr1"], 16)
            sync.dma_start(out=dcfm_h, in_=cfm_tmp).then_inc(S["cfm_s"], 16)
            sync.wait_ge(S["cbp"], NRT)
            sync.dma_start(out=AR5[0:1, 0:D], in_=b_h["gamma"][None, :]).then_inc(S["gbl"], 16)
            sync.dma_start(out=AR5[32:33, 0:D], in_=b_h["beta"][None, :]).then_inc(S["gbl"], 16)
            sync.wait_ge(S["sev"], 1)
            sync.dma_start(out=cc_in.ap(), in_=ssb).then_inc(S["sdm"], 16)
            for tc in range(NCH):
                sync.wait_ge(S["aof"], tc + 1)
                sync.dma_start(out=dao_h[P * tc:P * (tc + 1), :], in_=aotm[:, tc, :]).then_inc(S["ctw"], 16)
            sync.wait_ge(S["ctw"], 16 * (NCH + NCH))
            for fc in range(KC):
                sync.dma_start_transpose(aofm[:, fc, :], dao_h[:, P * fc:P * (fc + 1)]).then_inc(S["aotr"], 16)
            sync.wait_ge(S["aqr"], 128)
            for tc in range(2):
                sync.dma_start(out=xst2[:, tc % 2, :], in_=x_h[P * tc:P * (tc + 1), :]).then_inc(S["x2"], 16)
            for tc in range(NCH):
                sync.wait_ge(S["epo"], tc + 1)
                sync.dma_start(out=out_h[P * tc:P * (tc + 1), :], in_=ot[:, tc % 2, :]).then_inc(S["ost"], 16)
                if tc + 2 < NCH:
                    sync.dma_start(out=xst2[:, tc % 2, :],
                                   in_=x_h[P * (tc + 2):P * (tc + 3), :]).then_inc(S["x2"], 16)
            sync.wait_ge(S["ost"], 16 * NCH)

        @block.gpsimd
        def _(g):
            g.memset(epsc[:, 0:1], EPS_LN1)
            g.memset(epsc[:, 1:2], EPS_LN2)
            g.memset(epsc[:, 2:3], 1e-20).then_inc(S["cst"], 1)
            g.dma_start(out=ones_bf[0:1, :], in_=onesb_h[None, :]).then_inc(S["ld"], 16)
            g.dma_start(out=ones_bf[32:33, :], in_=onesb_h[None, :]).then_inc(S["ld"], 16)
            g.dma_start(out=ones_bf[64:65, :], in_=onesb_h[None, :]).then_inc(S["ld"], 16)
            g.dma_start(out=pack1[0:1, :], in_=b_h["bq"][None, :]).then_inc(S["ld"], 16)
            g.dma_start(out=pack1[32:33, :], in_=b_h["bk"][None, :]).then_inc(S["ld"], 16)
            g.dma_start(out=pack1[64:65, :], in_=b_h["bv"][None, :]).then_inc(S["ld"], 16)
            g.dma_start(out=pack2[0:1, :], in_=b_h["bq2"][None, :]).then_inc(S["ld"], 16)
            g.dma_start(out=pack2[32:33, :], in_=b_h["bk2"][None, :]).then_inc(S["ld"], 16)
            g.dma_start(out=pack2[64:65, :], in_=b_h["bv2"][None, :]).then_inc(S["ld"], 16)
            g.dma_start(out=borow[0:1, :], in_=b_h["bo"][None, :]).then_inc(S["ld"], 16)
            g.dma_start(out=scoreb[0:1, 0, :], in_=cbr_h[None, :]).then_inc(S["ld"], 16)
            g.dma_start(out=drt_h, in_=rw_h).then_inc(S["rtb"], 16)
            g.dma_start(out=wup[:, 0, :, :], in_=ffn_up_src(0)).then_inc(S["wu"], 16)
            g.dma_start(out=wdn[:, 0, :, :], in_=ffn_dn_src(0)).then_inc(S["wd"], 16)
            g.dma_start(out=wup[:, 1, :, :], in_=ffn_up_src(1)).then_inc(S["wu"], 16)

            u = 0
            for i in range(NEXP):
                if routed(i):
                    r = i - NSH
                    g.wait_ge(S["cbe"], r + 1)
                    for ti in range(16):
                        pi = u % 2
                        ts = ti % NTS
                        g.wait_ge(S["upf"], u + 1)
                        if u >= 2:
                            g.wait_ge(S["ubf"], u - 1)
                        g.tensor_tensor(ttmp[:, pi, :], rtmp[:, pi, :],
                                        cbt[:, r % 2, 512 * ts:512 * (ts + 1)],
                                        ALU.mult).then_inc(S["cm"], 1)
                        u += 1
                else:
                    u += 16
                if i + 1 < NEXP and routed(i + 1):
                    rr = (i + 1) - NSH
                    if rr == 0:
                        g.wait_ge(S["cfm_s"], 16)
                    if rr >= 2:
                        g.wait_ge(S["cbp"], rr - 1)
                    g.dma_start(out=crow[0:1, rr % 2, :], in_=dcfm_h[rr:rr + 1, :]).then_inc(S["crw"], 16)
                if i + 2 < NEXP:
                    g.wait_ge(S["upr"], 16 * (i + 1))
                    g.dma_start(out=wup[:, (i + 2) % 2, :, :], in_=ffn_up_src(i + 2)).then_inc(S["wu"], 16)
                if i + 1 < NEXP:
                    if i >= 1:
                        g.wait_ge(S["pedn"], ((i - 1) | 1) + 1)
                    g.dma_start(out=wdn[:, (i + 1) % 2, :, :], in_=ffn_dn_src(i + 1)).then_inc(S["wd"], 16)

            g.wait_ge(S["ubf"], N_UPT)
            for H in range(14):
                mat = ATTN_MATS[H // 2]
                half = H % 2
                if H >= 2:
                    g.wait_ge(S["ahc"], H - 1)
                if DEBUG_TAPS and H == 2:
                    g.wait_ge(S["awl"], 16)
                    g.dma_start(out=dbg_wr0, in_=wring[:, 0, :, :]).then_inc(S["dbg"], 16)
                if DEBUG_TAPS and H == 4:
                    g.wait_ge(S["aqf"], 32)
                    g.dma_start(out=dbg_pfm, in_=pfm).then_inc(S["dbg"], 16)
                g.dma_start(out=wring[:, H % 2, :, :],
                            in_=W_h[mat][:, 512 * half:512 * (half + 1)].rearrange(
                                "(c p) m -> p c m", p=P)).then_inc(S["awl"], 16)
            if DEBUG_TAPS:
                g.wait_ge(S["ate"], 64)
                g.dma_start(out=dbg_q2, in_=q2tm).then_inc(S["dbg"], 16)
                g.dma_start(out=dbg_k2, in_=k2tm).then_inc(S["dbg"], 16)
            g.wait_ge(S["sdm"], 16)
            g.collective_compute("AllReduce", ALU.add, replica_groups=[list(range(8))],
                                 ins=[cc_in.ap().opt()], outs=[cc_out.ap().opt()]).then_inc(S["col"], 1)
            g.wait_ge(S["col"], 1)
            g.memset(sbf2, 0.0)
            g.dma_start(out=sbf2.rearrange("p (j c) -> p j c", j=8)[0:64, :, 0:64],
                        in_=cc_out.ap().rearrange("p (j c) -> p j c", j=8)[0:64, :, :]).then_inc(S["sbfs"], 16)
            g.dma_start(out=sbf2.rearrange("p (j c) -> p j c", j=8)[64:128, :, 64:128],
                        in_=cc_out.ap().rearrange("p (j c) -> p j c", j=8)[64:128, :, :]).then_inc(S["sbfs"], 16)
            if DEBUG_TAPS:
                g.wait_ge(S["epo"], NCH)
                g.dma_start(out=dbg_xn, in_=dxn_h).then_inc(S["dbg"], 16)
                g.dma_start(out=dbg_cfm, in_=dcfm_h).then_inc(S["dbg"], 16)
                g.dma_start(out=dbg_g, in_=gtm).then_inc(S["dbg"], 16)
                g.dma_start(out=dbg_sbf, in_=sbf).then_inc(S["dbg"], 16)
                g.dma_start(out=dbg_ao, in_=aofm).then_inc(S["dbg"], 16)
                g.dma_start(out=dbg_qk, in_=qksr).then_inc(S["dbg"], 16)
                g.dma_start(out=dbg_ssb, in_=ssb).then_inc(S["dbg"], 16)
                g.dma_start(out=dbg_rtw, in_=rtw[:, :, :]).then_inc(S["dbg"], 16)
                g.dma_start(out=dbg_cbb, in_=cb_b[:, :]).then_inc(S["dbg"], 16)
                g.wait_ge(S["dbg"], 208)

        @block.tensor
        def _(t):
            t.wait_ge(S["ld"], 16 * 11)
            t.wait_ge(S["ldh"], 16 * 2)
            t.matmul(psA[:, 7, 0:NRT], ones_f[0:1, :], scoreb[0:1, 0, :],
                     start=True, stop=True).then_inc(S["cbb"], 1)
            t.wait_ge(S["xnf"], 16 * KC)
            t.wait_ge(S["rtw_s"], 16 * KC)
            for c in range(NCH):
                if c == 1:
                    t.wait_ge(S["cbb"], 2)
                if c >= 2:
                    t.wait_ge(S["rts"], c - 1)
                for kc in range(KC):
                    mm = t.matmul(psA[:, 6 + (c % 2), 0:NRT],
                                  xn[:, kc, P * c:P * (c + 1)], rtw[:, kc, :],
                                  start=(kc == 0), stop=(kc == KC - 1))
                mm.then_inc(S["rtl"], 1)

            u = 0
            d = 0
            for i in range(NEXP):
                t.wait_ge(S["wu"], 16 * (i + 1))
                if routed(i):
                    r = i - NSH
                    t.wait_ge(S["crw"], 16 * (r + 1))
                    if r == 0:
                        t.wait_ge(S["rts"], NCH)
                        t.wait_ge(S["cbb"], 2)
                    if r >= 2:
                        t.wait_ge(S["cbe"], r - 1)
                    for s4 in range(4):
                        mm = t.matmul(psA[:, 4 + s4, :], ones_bf[0:1, 0:P],
                                      crow[0:1, r % 2, 512 * s4:512 * (s4 + 1)],
                                      start=True, stop=True)
                    mm.then_inc(S["cbp"], 1)
                for mc in range(FUPC):
                    for ts in range(NTS):
                        pi = u % 2
                        if u >= 2:
                            t.wait_ge(S["upf"], u - 1)
                        for kc in range(KC):
                            mm = t.matmul(ups[:, pi, :],
                                          wup[:, i % 2, kc, P * mc:P * (mc + 1)],
                                          xn[:, kc, 512 * ts:512 * (ts + 1)],
                                          start=(kc == 0), stop=(kc == KC - 1))
                        mm.then_inc(S["upr"], 1)
                        u += 1
                if i % 2 == 1:
                    t.wait_ge(S["ubf"], 16 * (i + 1))
                    t.wait_ge(S["wd"], 16 * (i + 1))
                    for tc in range(NCH):
                        for sl in range(2):
                            pi = d % 2
                            if d >= 2:
                                t.wait_ge(S["dnf"], d - 1)
                            first = True
                            for ee in (i - 1, i):
                                for kc in range(FUPC):
                                    mm = t.matmul(dns[:, pi, :],
                                                  upc[:, ee % 2, kc, P * tc:P * (tc + 1)],
                                                  wdn[:, ee % 2, kc, 512 * sl:512 * (sl + 1)],
                                                  start=first,
                                                  stop=(ee == i and kc == FUPC - 1))
                                    first = False
                            mm.then_inc(S["dnr"], 1)
                            d += 1
                    t.nop().then_inc(S["pedn"], 2)

            a_ctr = [0]
            at_ctr = [0]

            def fm_mat(mat, mat_i, rhs_t):
                for half in range(2):
                    H = 2 * mat_i + half
                    t.wait_ge(S["awl"], 16 * (H + 1))
                    for fc_l in range(4):
                        fc = 4 * half + fc_l
                        for ts in range(NTS):
                            a = a_ctr[0]
                            pi = a % 2
                            if a < 2:
                                t.wait_ge(S["upf"], N_UPT)
                            else:
                                t.wait_ge(S["aqf"], a - 1)
                            for kc in range(KC):
                                t.matmul(ups[:, pi, :],
                                         wring[:, H % 2, kc, P * fc_l:P * (fc_l + 1)],
                                         rhs_t[:, kc, 512 * ts:512 * (ts + 1)],
                                         start=(kc == 0), stop=False)
                            blh, brh = bias_fm_ops(mat, fc)
                            t.matmul(ups[:, pi, :], blh, brh, start=False,
                                     stop=True).then_inc(S["aqr"], 1)
                            a_ctr[0] += 1
                    t.nop().then_inc(S["ahc"], 1)

            def tm_mat(mat, mat_i, wait_fm_all):
                for half in range(2):
                    H = 2 * mat_i + half
                    t.wait_ge(S["awl"], 16 * (H + 1))
                    for tc in range(NCH):
                        at = at_ctr[0]
                        pi = at % 2
                        if at < 2:
                            t.wait_ge(S["dnf"], N_DNT)
                        else:
                            t.wait_ge(S["atf"], at - 1)
                        if tc == 0:
                            t.wait_ge(S["aqf"], wait_fm_all)
                        for kc in range(KC):
                            t.matmul(dns[:, pi, :],
                                     pfm[:, kc, P * tc:P * (tc + 1)],
                                     wring[:, H % 2, kc, :],
                                     start=(kc == 0), stop=False)
                        blh, brh = bias_tm_ops(mat, half)
                        t.matmul(dns[:, pi, :], blh, brh, start=False,
                                 stop=True).then_inc(S["atr"], 1)
                        at_ctr[0] += 1
                    t.nop().then_inc(S["ahc"], 1)

            fm_mat("Wq", 0, xn)
            tm_mat("Wq2", 1, 32)
            fm_mat("Wk", 2, xn)
            tm_mat("Wk2", 3, 64)
            fm_mat("Wv", 4, xn)

            t.wait_ge(S["cbe"], NRT)
            t.wait_ge(S["ate"], 64)
            for h in range(NH):
                for tc in range(NCH):
                    mm = t.matmul(psA[64 * (h % 2):64 * (h % 2) + 64, 4,
                                      DH * (h // 2):DH * (h // 2) + DH],
                                  k2tm[:, tc, DH * h:DH * (h + 1)],
                                  q2tm[:, tc, DH * h:DH * (h + 1)],
                                  start=(tc == 0), stop=(tc == NCH - 1))
            mm.then_inc(S["spr"], 1)

            fm_mat("Wv2", 5, pfm)

            t.wait_ge(S["sbfs"], 32)
            t.wait_ge(S["aqf"], 128)
            t.wait_ge(S["atf"], 64)
            for tc in range(NCH):
                if tc >= 2:
                    t.wait_ge(S["aof"], tc - 1)
                for j in range(8):
                    base = 5 if tc % 2 == 0 else 2
                    bidx = base + j // 4
                    col = (P * j) % 512
                    mm = t.matmul(psA[:, bidx, col:col + P],
                                  xn[:, j, P * tc:P * (tc + 1)],
                                  sbf2[:, P * j:P * (j + 1)],
                                  start=True, stop=True)
                mm.then_inc(S["aor"], 1)

            t.wait_ge(S["gbl"], 32)
            for gb in range(2):
                for sl in range(2):
                    t.matmul(psA[:, 7, :], ones_f[32 * gb:32 * gb + 1, :],
                             AR5[32 * gb:32 * gb + 1, 512 * sl:512 * (sl + 1)],
                             start=True, stop=True).then_inc(S["gbr"], 1)
                    t.wait_ge(S["gbe"], 2 * gb + sl + 1)

            t.wait_ge(S["awl"], 16 * 14)
            t.wait_ge(S["aotr"], 16 * KC)
            t.wait_ge(S["gbe"], 4)
            for tc in range(NCH):
                if tc >= 2:
                    t.wait_ge(S["epy"], tc - 1)
                for sl in range(2):
                    wo_bank = (ups[:, sl, :] if tc % 2 == 0
                               else psA[:, 4 + 3 * sl, :])
                    for kc in range(KC):
                        t.matmul(wo_bank,
                                 aofm[:, kc, P * tc:P * (tc + 1)],
                                 wring[:, (12 + sl) % 2, kc, :],
                                 start=(kc == 0), stop=False)
                    blh, brh = bias_tm_ops("Wo", sl)
                    t.matmul(wo_bank, blh, brh, start=False,
                             stop=True).then_inc(S["wor"], 1)

        @block.scalar
        def _(s):
            s.wait_ge(S["cst"], 1)
            for c in range(NCH):
                s.wait_ge(S["lnst"], c + 1)
                s.activation(stats[:, 0:1], r1[:, c:c + 1], AF.Ln, bias=epsc[:, 0:1])
                s.activation(r1[:, c:c + 1], stats[:, 0:1], AF.Exp, scale=-0.5).then_inc(S["lnr"], 1)
            for c in range(NCH):
                s.wait_ge(S["rtl"], c + 1)
                if c >= 2:
                    s.wait_ge(S["rtc"], c - 1)
                s.activation(scoreb[:, c % 2, :], psA[:, 6 + (c % 2), 0:NRT],
                             AF.Sigmoid).then_inc(S["rts"], 1)
                s.wait_ge(S["rtd"], c + 1)
                s.activation(stats[:, 1:2], dcol[:, c % 2:c % 2 + 1], AF.Ln, bias=epsc[:, 2:3])
                s.activation(rcol[:, c % 2:c % 2 + 1], stats[:, 1:2], AF.Exp,
                             scale=-1.0).then_inc(S["rtr"], 1)
            u = 0
            for i in range(NEXP):
                if routed(i):
                    r = i - NSH
                    s.wait_ge(S["cbp"], r + 1)
                    if r >= 2:
                        s.wait_ge(S["cm"], 16 * (r - 1))
                    for s4 in range(4):
                        cp = s.copy(cbt[:, r % 2, 512 * s4:512 * (s4 + 1)], psA[:, 4 + s4, :])
                    cp.then_inc(S["cbe"], 1)
                for ti in range(16):
                    pi = u % 2
                    s.wait_ge(S["upr"], u + 1)
                    if u >= 2:
                        s.wait_ge(S["ubf"], u - 1)
                    s.activation(rtmp[:, pi, :], ups[:, pi, :], AF.Relu).then_inc(S["upf"], 1)
                    u += 1
            s.wait_ge(S["dnf"], N_DNT)
            for tc in range(NCH):
                s.activation(gtm[:, tc, :], acc[:, tc, :], AF.Silu).then_inc(S["gsl"], 1)

            a_ctr = [0]
            t_ctr = [0]

            def fm_evac(tgt, pre_wait=None):
                for half in range(2):
                    for fc_l in range(4):
                        fc = 4 * half + fc_l
                        for ts in range(NTS):
                            a = a_ctr[0]
                            pi = a % 2
                            s.wait_ge(S["aqr"], a + 1)
                            if pre_wait is not None and half == 0 and fc_l == 0 and ts == 0:
                                s.wait_ge(S[pre_wait[0]], pre_wait[1])
                            s.copy(tgt[:, fc, 512 * ts:512 * (ts + 1)],
                                   ups[:, pi, :]).then_inc(S["aqf"], 1)
                            a_ctr[0] += 1

            def tm_elu_act():
                for k in range(32):
                    ti = t_ctr[0]
                    p2 = ti % 2
                    s.wait_ge(S["atr"], ti + 1)
                    if ti >= 2:
                        s.wait_ge(S["ate"], ti - 1)
                    s.activation(scrA[:, 2 * p2, :], dns[:, p2, :], AF.Relu).then_inc(S["er1"], 1)
                    s.wait_ge(S["atf"], ti + 1)
                    s.activation(scrA[:, 2 * p2 + 1, :], scrA[:, 2 * p2 + 1, :],
                                 AF.Exp).then_inc(S["er2"], 1)
                    t_ctr[0] += 1

            fm_evac(pfm)
            tm_elu_act()
            fm_evac(pfm, ("atr", 32))
            tm_elu_act()
            fm_evac(pfm, ("atr", 64))
            fm_evac(xn)

            for tc in range(NCH):
                s.wait_ge(S["eps"], tc + 1)
                s.activation(stats[:, 3:4], stats[:, 5:6], AF.Ln, bias=epsc[:, 1:2])
                s.activation(stats[:, 4:5], stats[:, 3:4], AF.Exp, scale=-0.5).then_inc(S["epr"], 1)

        @block.vector
        def _(v):
            fcnt = [0]

            def fence(op):
                fcnt[0] += 1
                op.then_inc(S["fnc"], 1)
                v.wait_ge(S["fnc"], fcnt[0])

            v.wait_ge(S["cbb"], 1)
            v.tensor_copy(cb_b[:, :], psA[:, 7, 0:NRT]).then_inc(S["cbb"], 1)
            for c in range(NCH):
                v.wait_ge(S["x1"], 16 * (c + 1))
                xc = xst[:, c % 4, :]
                v.reduce_sum(stats[:, 8:9], xc, axis=AX.X)
                fence(v.scalar_tensor_tensor(xsq, xc, 1.0, xc, ALU.mult, ALU.mult,
                                             accum_out=stats[:, 9:10]))
                v.tensor_scalar_mul(mu1[:, c:c + 1], stats[:, 8:9], 1.0 / D)
                v.tensor_scalar_mul(stats[:, 10:11], stats[:, 9:10], 1.0 / D)
                fence(v.tensor_tensor(stats[:, 11:12], mu1[:, c:c + 1], mu1[:, c:c + 1], ALU.mult))
                v.tensor_tensor(r1[:, c:c + 1], stats[:, 10:11], stats[:, 11:12],
                                ALU.subtract).then_inc(S["lnst"], 1)
                v.wait_ge(S["lnr"], c + 1)
                if c >= 2:
                    v.wait_ge(S["xnw"], 16 * (c - 1))
                v.tensor_scalar(xnt[:, c % 2, :], xc, mu1[:, c:c + 1], r1[:, c:c + 1],
                                ALU.subtract, ALU.mult).then_inc(S["lnx"], 1)
            for c in range(NCH):
                v.wait_ge(S["rts"], c + 1)
                sc = scoreb[:, c % 2, :]
                fence(v.tensor_tensor(r_sfc, sc, cb_b[:, :], ALU.add))
                for gi in range(GEXP):
                    op = v.max(r_gsort[:, 8 * gi:8 * gi + 8], r_sfc[:, 8 * gi:8 * gi + 8])
                fence(op)
                gs3 = r_gsort.rearrange("p (g j) -> p g j", g=GEXP)
                fence(v.tensor_tensor(r_gsc, gs3[:, :, 0], gs3[:, :, 1], ALU.add))
                fence(v.max(r_gss, r_gsc))
                fence(v.tensor_single_scalar(r_gm, r_gsc, r_gss[:, 3:4], ALU.is_ge))
                for gi in range(GEXP):
                    op = v.tensor_single_scalar(r_msk[:, 8 * gi:8 * gi + 8],
                                                r_sfc[:, 8 * gi:8 * gi + 8],
                                                r_gm[:, gi:gi + 1], ALU.mult)
                fence(op)
                fence(v.max(r_top8, r_msk))
                fence(v.tensor_single_scalar(r_tm, r_msk, r_top8[:, 7:8], ALU.is_ge))
                fence(v.tensor_tensor(r_sel, r_tm, sc, ALU.mult))
                v.reduce_sum(dcol[:, c % 2:c % 2 + 1], r_sel, axis=AX.X).then_inc(S["rtd"], 1)
                v.wait_ge(S["rtr"], c + 1)
                if c >= 2:
                    v.wait_ge(S["ctw"], 16 * (c - 1))
                v.memset(ctmb[:, c % 2, NRT:P], 0.0)
                v.tensor_scalar(ctmb[:, c % 2, 0:NRT], r_sel, rcol[:, c % 2:c % 2 + 1],
                                SCALE_MOE, ALU.mult, ALU.mult).then_inc(S["rtc"], 1)

            v.wait_ge(S["xnw"], 16 * NCH)
            v.memset(acc, 0.0)

            u = 0
            d = 0
            for i in range(NEXP):
                rt = routed(i)
                for mc in range(FUPC):
                    for ts in range(NTS):
                        pi = u % 2
                        if mc == 0 and ts == 0 and i >= 2:
                            v.wait_ge(S["pedn"], i - (i % 2))
                        if rt:
                            v.wait_ge(S["cm"], 16 * (i - NSH) + mc * NTS + ts + 1)
                            src2 = ttmp[:, pi, :]
                        else:
                            v.wait_ge(S["upf"], u + 1)
                            src2 = rtmp[:, pi, :]
                        v.tensor_tensor(upc[:, i % 2, mc, 512 * ts:512 * (ts + 1)],
                                        rtmp[:, pi, :], src2, ALU.mult).then_inc(S["ubf"], 1)
                        u += 1
                if i % 2 == 1:
                    for tc in range(NCH):
                        for sl in range(2):
                            pi = d % 2
                            v.wait_ge(S["dnr"], d + 1)
                            v.tensor_tensor(acc[:, tc, 512 * sl:512 * (sl + 1)],
                                            acc[:, tc, 512 * sl:512 * (sl + 1)],
                                            dns[:, pi, :], ALU.add).then_inc(S["dnf"], 1)
                            d += 1

            for ti in range(64):
                p2 = ti % 2
                if ti < 32:
                    tgt, sl, tc = q2tm, ti // 16, ti % 16
                else:
                    tgt, sl, tc = k2tm, (ti - 32) // 16, (ti - 32) % 16
                v.wait_ge(S["er1"], ti + 1)
                v.tensor_tensor(scrA[:, 2 * p2 + 1, :], dns[:, p2, :], scrA[:, 2 * p2, :],
                                ALU.subtract).then_inc(S["atf"], 1)
                v.wait_ge(S["er2"], ti + 1)
                v.tensor_tensor(tgt[:, tc, 512 * sl:512 * (sl + 1)],
                                scrA[:, 2 * p2, :], scrA[:, 2 * p2 + 1, :],
                                ALU.add).then_inc(S["ate"], 1)

            v.wait_ge(S["ate"], 64)
            for tc in range(NCH):
                fence(v.tensor_tensor(qscr, q2tm[:, tc, :], k2tm[:, tc, :], ALU.mult))
                for h in range(NH):
                    op = v.reduce_sum(qks[:, tc, h:h + 1], qscr[:, DH * h:DH * (h + 1)],
                                      axis=AX.X)
            fence(op)
            v.tensor_scalar_add(qks.rearrange("p a b -> p (a b)"),
                                qks.rearrange("p a b -> p (a b)"), EPS_QK)
            fence(v.reciprocal(qksr.rearrange("p a b -> p (a b)"),
                               qks.rearrange("p a b -> p (a b)")))
            v.wait_ge(S["spr"], 1)
            v.tensor_copy(ssb, psA[:, 4, :]).then_inc(S["sev"], 1)
            for tc in range(NCH):
                v.wait_ge(S["aor"], tc + 1)
                bank = psA[:, 5:7, :] if tc % 2 == 0 else psA[:, 2:4, :]
                bank = bank.rearrange("p a b -> p (a b)")
                for h in range(NH):
                    op = v.tensor_single_scalar(aotm[:, tc, DH * h:DH * (h + 1)],
                                                bank[:, DH * h:DH * (h + 1)],
                                                qksr[:, tc, h:h + 1], ALU.mult)
                op.then_inc(S["aof"], 1)
            v.wait_ge(S["sdm"], 16)
            for gb in range(2):
                tgt = gamb if gb == 0 else betb
                for sl in range(2):
                    v.wait_ge(S["gbr"], 2 * gb + sl + 1)
                    v.tensor_copy(tgt[:, 512 * sl:512 * (sl + 1)], psA[:, 7, :]).then_inc(S["gbe"], 1)

            for tc in range(NCH):
                v.wait_ge(S["wor"], 2 * (tc + 1))
                v.wait_ge(S["x2"], 16 * (tc + 1))
                fence(v.tensor_scalar(scT, xst2[:, tc % 2, :], mu1[:, tc:tc + 1], r1[:, tc:tc + 1],
                                      ALU.subtract, ALU.mult))
                if tc % 2 == 0:
                    wo_ps = psA[:, 0:2, :].rearrange("p a b -> p (a b)")
                else:
                    wo_ps = None
                if tc % 2 == 0:
                    v.tensor_tensor(yt, scT, wo_ps, ALU.add).then_inc(S["epy"], 1)
                else:
                    v.tensor_tensor(yt[0:P, 0:512], scT[0:P, 0:512], psA[:, 4, :], ALU.add)
                    v.tensor_tensor(yt[0:P, 512:1024], scT[0:P, 512:1024],
                                    psA[:, 7, :], ALU.add).then_inc(S["epy"], 1)
                v.wait_ge(S["epy"], tc + 1)
                v.reduce_sum(stats[:, 6:7], yt, axis=AX.X)
                fence(v.scalar_tensor_tensor(scE, yt, 1.0, yt, ALU.mult, ALU.mult,
                                             accum_out=stats[:, 7:8]))
                v.tensor_scalar_mul(stats[:, 12:13], stats[:, 6:7], 1.0 / D)
                v.tensor_scalar_mul(stats[:, 13:14], stats[:, 7:8], 1.0 / D)
                v.tensor_tensor(stats[:, 14:15], stats[:, 12:13], stats[:, 12:13], ALU.mult)
                v.tensor_tensor(stats[:, 5:6], stats[:, 13:14], stats[:, 14:15],
                                ALU.subtract).then_inc(S["eps"], 1)
                v.wait_ge(S["epr"], tc + 1)
                if tc == 0:
                    v.wait_ge(S["gbe"], 4)
                    v.wait_ge(S["gsl"], NCH)
                fence(v.tensor_single_scalar(rgt, gamb, stats[:, 4:5], ALU.mult))
                fence(v.scalar_tensor_tensor(scT, yt, stats[:, 12:13], rgt, ALU.subtract, ALU.mult))
                fence(v.tensor_tensor(scT, scT, betb, ALU.add))
                fence(v.tensor_tensor(scE, scT, gtm[:, tc, :], ALU.mult))
                if tc >= 2:
                    v.wait_ge(S["ost"], 16 * (tc - 1))
                if DEBUG_EPI == 1:
                    v.tensor_copy(ot[:, tc % 2, :], yt).then_inc(S["epo"], 1)
                elif DEBUG_EPI == 2:
                    v.tensor_copy(ot[:, tc % 2, :], scT).then_inc(S["epo"], 1)
                else:
                    v.tensor_tensor(ot[:, tc % 2, :], scE, xst2[:, tc % 2, :],
                                    ALU.add).then_inc(S["epo"], 1)

    es.close()
    return nc


_NC_CACHE = None


def kernel(L0_attn_x, L0_attn_Wq, L0_attn_bq, L0_attn_Wk, L0_attn_bk, L0_attn_Wv,
           L0_attn_bv, L0_attn_Wq2, L0_attn_bq2, L0_attn_Wk2, L0_attn_bk2,
           L0_attn_Wv2, L0_attn_bv2, L0_attn_Wo, L0_attn_bo, L0_attn_gamma,
           L0_attn_beta, L0_ffn_router_w, L0_ffn_correction_bias, L0_ffn_up_w,
           L0_ffn_down_w, L0_ffn_shared_up_w, L0_ffn_shared_down_w):
    global _NC_CACHE
    if _NC_CACHE is None:
        _NC_CACHE = build_nc()
    nc = _NC_CACHE

    f32c = lambda a: np.ascontiguousarray(np.asarray(a, dtype=np.float32))
    xfull = f32c(L0_attn_x).reshape(8 * T, D)
    com = {
        "Wq": f32c(L0_attn_Wq), "Wk": f32c(L0_attn_Wk), "Wv": f32c(L0_attn_Wv),
        "Wq2": f32c(L0_attn_Wq2), "Wk2": f32c(L0_attn_Wk2), "Wv2": f32c(L0_attn_Wv2),
        "Wo": f32c(L0_attn_Wo),
        "bq": f32c(L0_attn_bq), "bk": f32c(L0_attn_bk), "bv": f32c(L0_attn_bv),
        "bq2": f32c(L0_attn_bq2), "bk2": f32c(L0_attn_bk2), "bv2": f32c(L0_attn_bv2),
        "bo": f32c(L0_attn_bo), "gamma": f32c(L0_attn_gamma), "beta": f32c(L0_attn_beta),
        "router_w": f32c(L0_ffn_router_w), "corr_bias": f32c(L0_ffn_correction_bias),
        "up_w": f32c(L0_ffn_up_w), "down_w": f32c(L0_ffn_down_w),
        "shared_up": f32c(L0_ffn_shared_up_w), "shared_down": f32c(L0_ffn_shared_down_w),
        "ones_bf": np.ones((512,), dtype=np.float16),
        "ones_f": np.ones((P,), dtype=np.float32),
    }
    in_maps = []
    for core in range(8):
        m = dict(com)
        m["x"] = np.ascontiguousarray(xfull[T * core:T * (core + 1)])
        in_maps.append(m)
    res = run_bass_kernel_spmd(nc, in_maps, core_ids=list(range(8)))
    out = np.concatenate([np.asarray(r["out"]) for r in res.results], axis=0)
    return out.reshape(1, 8 * T, D).astype(np.float32)
